# version 02
# baseline (speedup 1.0000x reference)
import contextlib
import numpy as np
import concourse.bass as bass
import concourse.mybir as mybir
from concourse.bass_utils import run_bass_kernel_spmd

F32 = mybir.dt.float32
AF = mybir.ActivationFunctionType
ALU = mybir.AluOpType
AX = mybir.AxisListType

D = 4096
DC = 32
H = 32
DH = 128
B = 2
S = 8192
NCORES = 8
TOK = 2048
TT = 256
NT = TOK // TT
HALO = 256
AR = TOK + 2 * HALO
NSUB = AR // TT
SCALE = 1.0 / float(np.sqrt(128.0))
EPS = 1e-5
INV_D = 1.0 / float(D)

NT_BUILD = NT
DEBUG_DUMP = False
DBG_HEADS = H
DEBUG_EARLY = False
DBG_FFN = True
DBG_Y = True


class Sem:
    __slots__ = ("h", "n")

    def __init__(self, h):
        self.h = h
        self.n = 0


class Prog:
    ENGINES = ("tensor", "scalar", "vector", "gpsimd", "sync")

    def __init__(self):
        self.ops = {e: [] for e in self.ENGINES}
        self.lastwait = {e: {} for e in self.ENGINES}
        self.chain_sem = {}

    def add(self, eng, fn, waits=(), inc=None, inck=1):
        cs = self.chain_sem.get(eng)
        waits = list(waits)
        if cs is not None and fn is not None:
            waits.append((cs, cs.n))
            if inc is None:
                inc = cs
        lw = self.lastwait[eng]
        ws = []
        for sem, c in waits:
            if c is None or c <= 0:
                continue
            assert c <= sem.n, f"wait on future count {c} > {sem.n} ({eng})"
            if lw.get(id(sem), 0) < c:
                ws.append((sem, c))
                lw[id(sem)] = c
        self.ops[eng].append((ws, fn, inc, inck))
        if inc is not None:
            inc.n += inck
            return inc.n
        return None

    def emit(self, name, eng):
        for ws, fn, inc, k in self.ops[name]:
            for sem, c in ws:
                eng.wait_ge(sem.h, c)
            ins = fn(eng) if fn is not None else None
            if inc is not None:
                if ins is None:
                    ins = eng.nop()
                ins.then_inc(inc.h, k)


def build_nc():
    nc = bass.Bass("TRN2", target_bir_lowering=False)

    def din(name, shape):
        return nc.dram_tensor(name, list(shape), F32, kind="ExternalInput").ap()

    xs_p = din("xs_p", (NSUB, 128, DC, TT))
    cos_p = din("cos_p", (NSUB, 128, TT))
    sin_p = din("sin_p", (NSUB, 128, TT))
    wk_p = din("wk_p", (128, DC, 128))
    wv_p = din("wv_p", (128, DC, 128))
    wq_p = din("wq_p", (H, 128, DC, 128))
    wo_p = din("wo_p", (DC, 128, DC + 1, 128))
    wb_p = din("wb_p", (DC, 128, DC, 128))
    at_p = din("at_p", (128, DC, 128))
    wr_p = din("wr_p", (128, DC, 8))
    bb_p = din("bb_p", (128, DC))
    coef_p = din("coef_p", (128, NT * 4))
    sub_p = din("sub_p", (128, NT * 4))
    rm_p = din("rm_p", (128, 128))
    im_p = din("im_p", (128, 128))
    em_p = din("em_p", (8, 128))
    ones1_p = din("ones1_p", (1, 128))
    onesc_p = din("onesc_p", (128, 1))
    eps_p = din("eps_p", (1, 1))
    can_p = din("can_p", (128, 256))

    y_out = nc.dram_tensor("y_out", [NT, DC, 128, TT], F32, kind="ExternalOutput").ap()

    with contextlib.ExitStack() as ctx:
        def sb(name, shape):
            return ctx.enter_context(nc.sbuf_tensor(name, list(shape), F32))

        def psum(name):
            return ctx.enter_context(nc.psum_tensor(name, [128, 512], F32))

        def sem(name):
            return Sem(ctx.enter_context(nc.semaphore(name)))

        x_sb = sb("x_sb", (128, DC, TT))
        xn_sb = sb("xn_sb", (128, DC, TT))
        ao_sb = sb("ao_sb", (128, H, TT))
        can1 = sb("can1", (128, 256))
        kr_sb = sb("kr_sb", (128, AR))
        can2 = sb("can2", (128, 256))
        v_sb = sb("v_sb", (128, AR))
        can3 = sb("can3", (128, 256))
        wbufs = [sb(f"wbuf{i}", (128, DC + 1, 128)) for i in range(3)]
        cs_sb = sb("cs_sb", (128, TT))
        sn_sb = sb("sn_sb", (128, TT))
        qt1 = sb("qt1", (128, TT))
        tmpA = sb("tmpA", (128, TT))
        tmpB = sb("tmpB", (128, TT))
        qr_sb = sb("qr_sb", (128, TT))
        sq_sb = sb("sq_sb", (128, TT))
        an = [sb(f"an{i}", (128, 512)) for i in range(2)]
        asc = [sb(f"asc{i}", (128, 512)) for i in range(2)]
        aT = [sb(f"aT{i}", (128, 512)) for i in range(2)]
        scol = sb("scol", (128, 2))
        sdum = sb("sdum", (128, 1))
        ctmp = sb("ctmp", (128, 1))
        ctmp2 = sb("ctmp2", (128, 1))
        ccol = sb("ccol", (128, 1))
        ra = sb("ra", (1, TT))
        rb = sb("rb", (1, TT))
        rc = sb("rc", (1, TT))
        rd = sb("rd", (1, TT))
        rn = sb("rn", (1, TT))
        ro = sb("ro", (1, TT))
        rr_t = sb("rr_t", (1, TT))
        m2_t = sb("m2_t", (1, TT))
        lg_sb = sb("lg_sb", (8, TT))
        lgt2 = [sb("lgt0", (128, 8)), sb("lgt1", (128, 8))]
        mk1 = sb("mk1", (128, 8))
        tb8 = sb("tb8", (128, 8))
        l2 = sb("l2", (128, 8))
        mk2 = sb("mk2", (128, 8))
        m1c = sb("m1c", (128, 1))
        m2c = sb("m2c", (128, 1))
        dcol = sb("dcol", (128, 1))
        p1c = sb("p1c", (128, 1))
        p2c = sb("p2c", (128, 1))
        w1c = sb("w1c", (128, 8))
        w2c = sb("w2c", (128, 8))
        wtok = sb("wtok", (128, 8))
        wfm = sb("wfm", (8, TT))
        wexp_sb = sb("wexp_sb", (128, TT))
        hw_sb = sb("hw_sb", (128, TT))
        ya = [sb(f"ya{i}", (128, TT)) for i in range(2)]
        yb = [sb(f"yb{i}", (128, TT)) for i in range(2)]
        wr_sb = sb("wr_sb", (128, DC, 8))
        bb_sb = sb("bb_sb", (128, DC))
        cf_sb = sb("cf_sb", (128, NT * 4))
        su_sb = sb("su_sb", (128, NT * 4))
        rm_sb = sb("rm_sb", (128, 128))
        im_sb = sb("im_sb", (128, 128))
        em_sb = sb("em_sb", (8, 128))
        o1_sb = sb("o1_sb", (1, 128))
        oc_sb = sb("oc_sb", (128, 1))
        eps_sb = sb("eps_sb", (1, 1))

        p0 = psum("p0")
        p1 = psum("p1")
        p2 = psum("p2")
        p3 = psum("p3")
        p4 = psum("p4")
        p5 = psum("p5")
        p6 = psum("p6")
        p7 = psum("p7")
        pS = [p1, p2]
        pY = [p6, p7]

        TE = sem("TE")
        SE = sem("SE")
        VE = sem("VE")
        LDC = sem("LDC")
        LDWs = [sem(f"LDW{i}") for i in range(3)]
        LDXX = sem("LDXX")
        LDXC = sem("LDXC")
        STOs = [sem("STO0"), sem("STO1")]
        DBG = sem("DBG")

        pr = Prog()
        pr.chain_sem["vector"] = VE
        pr.chain_sem["scalar"] = SE

        ldw_c = {}
        for nm, dst, src in [
            ("wr", wr_sb, wr_p), ("bb", bb_sb, bb_p),
            ("cf", cf_sb, coef_p), ("su", su_sb, sub_p), ("rm", rm_sb, rm_p),
            ("im", im_sb, im_p), ("em", em_sb, em_p), ("o1", o1_sb, ones1_p),
            ("oc", oc_sb, onesc_p), ("eps", eps_sb, eps_p),
            ("c1", can1, can_p), ("c2", can2, can_p), ("c3", can3, can_p),
        ]:
            pr.add("sync",
                   (lambda d, s: (lambda e: e.dma_start(out=d[:], in_=s)))(dst, src),
                   inc=LDC, inck=16)
        for nm in list(ldw_c) or []:
            pass
        _all_c = LDC.n
        for nm in ("wr", "bb", "cf", "su", "rm", "im", "em", "o1", "oc", "eps",
                   "c1", "c2", "c3"):
            ldw_c[nm] = None
        ldw_c = {k: (LDC, _all_c) for k in ldw_c}

        wstream = [(wk_p, DC), (wv_p, DC)]
        for _t in range(NT_BUILD):
            wstream.extend((wq_p[h], DC) for h in range(DBG_HEADS))
            if DBG_FFN:
                wstream.append((at_p, DC))
            if DBG_Y:
                for oc in range(DC):
                    wstream.append((wo_p[oc], DC + 1))
                    wstream.append((wb_p[oc], DC))
        ldw_blk = [None] * len(wstream)
        freew_blk = [None] * len(wstream)
        next_load = [0]

        def wload_next():
            k = next_load[0]
            if k >= len(wstream):
                return
            next_load[0] += 1
            waits = []
            if k >= 3:
                assert freew_blk[k - 3] is not None, f"block {k-3} not yet freed"
                waits.append((TE, freew_blk[k - 3]))
            src, ncol = wstream[k]
            buf = wbufs[k % 3]
            s_sem = LDWs[k % 3]
            dst = buf[:] if ncol == DC + 1 else buf[:, 0:DC]
            cnt = pr.add("sync",
                         (lambda d_, s_: (lambda e: e.dma_start(out=d_, in_=s_)))(dst, src),
                         waits=waits, inc=s_sem, inck=16)
            ldw_blk[k] = (s_sem, cnt)

        wload_next(); wload_next(); wload_next()

        def load_x(sub, waits):
            c = pr.add("gpsimd", lambda e, s=sub: e.dma_start(out=x_sb[:], in_=xs_p[s]),
                       waits=waits, inc=LDXX, inck=16)
            return (LDXX, c)

        def load_cs(sub, waits):
            pr.add("gpsimd", lambda e, s=sub: e.dma_start(out=cs_sb[:], in_=cos_p[s]),
                   waits=waits, inc=LDXC, inck=16)
            c2 = pr.add("gpsimd", lambda e, s=sub: e.dma_start(out=sn_sb[:], in_=sin_p[s]),
                        inc=LDXC, inck=16)
            return (LDXC, c2)

        st = dict(sq_free=0, p3v_free=0, p3s_free=0, p4_free=0, rows_free=0)

        def emit_ln(xc, extra_bcast_waits=(), p5_waits=()):
            mm_sq = None
            for dc in range(DC):
                sqw = [xc]
                if st["sq_free"]:
                    sqw.append((TE, st["sq_free"]))
                sq_c = pr.add("scalar",
                              lambda e, d=dc: e.activation(out=sq_sb[:], in_=x_sb[:, d],
                                                           func=AF.Square),
                              waits=sqw, inc=SE)
                pr.add("tensor",
                       lambda e, d=dc: e.matmul(p3[0:1, 0:TT], oc_sb[:], x_sb[:, d],
                                                start=(d == 0), stop=(d == DC - 1)),
                       waits=[xc, ldw_c["oc"],
                              (VE, st["p3v_free"]), (SE, st["p3s_free"])])
                mm_sq = pr.add("tensor",
                               lambda e, d=dc: e.matmul(p5[0:1, 0:TT], oc_sb[:], sq_sb[:],
                                                        start=(d == 0), stop=(d == DC - 1)),
                               waits=[(SE, sq_c)] + (list(p5_waits) if dc == 0 else []),
                               inc=TE)
                st["sq_free"] = mm_sq
            pr.add("vector", lambda e: e.tensor_scalar(out=ra[:], in0=p3[0:1, 0:TT],
                                                       scalar1=INV_D, scalar2=None, op0=ALU.mult),
                   waits=[(TE, mm_sq), (TE, st["rows_free"])])
            v2 = pr.add("vector", lambda e: e.tensor_scalar(out=rb[:], in0=p5[0:1, 0:TT],
                                                            scalar1=INV_D, scalar2=None,
                                                            op0=ALU.mult),
                        inc=VE)
            st["p3v_free"] = v2
            pr.add("vector", lambda e: e.tensor_tensor(out=rc[:], in0=ra[:], in1=ra[:], op=ALU.mult))
            v4 = pr.add("vector", lambda e: e.tensor_tensor(out=rd[:], in0=rb[:], in1=rc[:],
                                                            op=ALU.subtract),
                        inc=VE)
            sqr = pr.add("scalar", lambda e: e.activation(out=rc[:], in_=rd[:], func=AF.Sqrt,
                                                          bias=eps_sb[:]),
                         waits=[(VE, v4), ldw_c["eps"]], inc=SE)
            pr.add("vector", lambda e: e.tensor_scalar(out=rb[:], in0=rd[:], scalar1=EPS,
                                                       scalar2=None, op0=ALU.add),
                   waits=[(SE, sqr)])
            pr.add("vector", lambda e: e.reciprocal(rn[:], rc[:]))
            pr.add("vector", lambda e: e.tensor_tensor(out=rc[:], in0=rn[:], in1=rn[:], op=ALU.mult))
            pr.add("vector", lambda e: e.tensor_tensor(out=rd[:], in0=rc[:], in1=rb[:], op=ALU.mult))
            pr.add("vector", lambda e: e.tensor_scalar(out=rc[:], in0=rd[:], scalar1=-0.5,
                                                       scalar2=1.5, op0=ALU.mult, op1=ALU.add))
            pr.add("vector", lambda e: e.tensor_tensor(out=ro[:], in0=rn[:], in1=rc[:], op=ALU.mult))
            pr.add("vector", lambda e: e.tensor_tensor(out=rc[:], in0=ro[:], in1=ro[:], op=ALU.mult))
            pr.add("vector", lambda e: e.tensor_tensor(out=rd[:], in0=rc[:], in1=rb[:], op=ALU.mult))
            pr.add("vector", lambda e: e.tensor_scalar(out=rc[:], in0=rd[:], scalar1=-0.5,
                                                       scalar2=1.5, op0=ALU.mult, op1=ALU.add))
            pr.add("vector", lambda e: e.tensor_tensor(out=rr_t[:], in0=ro[:], in1=rc[:], op=ALU.mult))
            rows_done = pr.add("vector", lambda e: e.tensor_tensor(out=m2_t[:], in0=ra[:],
                                                                   in1=rr_t[:], op=ALU.mult),
                               inc=VE)
            pr.add("tensor", lambda e: e.matmul(p4[:, 0:TT], o1_sb[:], rr_t[:],
                                                start=True, stop=True),
                   waits=[(VE, rows_done), ldw_c["o1"],
                          (VE, st["p4_free"])] + list(extra_bcast_waits))
            bcast_done = pr.add("tensor", lambda e: e.matmul(p4[:, TT:2 * TT], o1_sb[:], m2_t[:],
                                                             start=True, stop=True),
                                inc=TE)
            st["rows_free"] = bcast_done
            xn_done = None
            for dc in range(DC):
                pr.add("vector", lambda e, d=dc: e.tensor_tensor(out=tmpA[:], in0=x_sb[:, d],
                                                                 in1=p4[:, 0:TT], op=ALU.mult),
                       waits=[(TE, bcast_done), xc])
                xn_done = pr.add("vector",
                                 lambda e, d=dc: e.tensor_tensor(out=xn_sb[:, d], in0=tmpA[:],
                                                                 in1=p4[:, TT:2 * TT],
                                                                 op=ALU.subtract),
                                 inc=VE)
            st["p4_free"] = xn_done
            return xn_done, bcast_done

        x_free = 0
        cs_free = 0
        p0_free = 0
        p1_free = 0
        p2_free = 0
        kv_ready = 0

        for i in range(NSUB):
            xc = load_x(i, [(VE, x_free)])
            cc = load_cs(i, [(VE, cs_free)])
            xn_done, _ = emit_ln(xc, p5_waits=[(VE, st["p3v_free"])])
            x_free = xn_done

            kmm = None
            for dc in range(DC):
                kmm = pr.add("tensor",
                             lambda e, d=dc: e.matmul(p0[:, 0:TT], wbufs[0][:, d], xn_sb[:, d],
                                                      start=(d == 0), stop=(d == DC - 1)),
                             waits=[ldw_blk[0], (VE, xn_done), (SE, p0_free)],
                             inc=TE if dc == DC - 1 else None)
            if i == NSUB - 1:
                freew_blk[0] = kmm
                wload_next()
            kev = pr.add("scalar", lambda e: e.activation(out=qt1[:], in_=p0[:, 0:TT], func=AF.Copy),
                         waits=[(TE, kmm), (VE, cs_free)], inc=SE)
            p0_free = kev
            rot = pr.add("tensor", lambda e: e.matmul(p1[:, 0:TT], rm_sb[:], qt1[:],
                                                      start=True, stop=True),
                         waits=[(SE, kev), ldw_c["rm"], (VE, p1_free)], inc=TE)
            pr.add("vector", lambda e: e.tensor_tensor(out=tmpA[:], in0=qt1[:], in1=cs_sb[:],
                                                       op=ALU.mult),
                   waits=[(SE, kev), cc])
            pr.add("vector", lambda e: e.tensor_tensor(out=tmpB[:], in0=sn_sb[:], in1=p1[:, 0:TT],
                                                       op=ALU.mult),
                   waits=[(TE, rot)])
            krope = pr.add("vector", lambda e, i_=i: e.tensor_tensor(
                out=kr_sb[:, i_ * TT:(i_ + 1) * TT], in0=tmpA[:], in1=tmpB[:], op=ALU.add),
                inc=VE)
            p1_free = krope
            cs_free = krope

            vmm = None
            for tc in range(TT // 128):
                for dc in range(DC):
                    vmm = pr.add("tensor",
                                 lambda e, d=dc, t_=tc: e.matmul(
                                     p2[:, t_ * 128:(t_ + 1) * 128],
                                     xn_sb[:, d, t_ * 128:(t_ + 1) * 128],
                                     wbufs[1][:, d],
                                     start=(d == 0), stop=(d == DC - 1)),
                                 waits=[ldw_blk[1], (VE, xn_done), (VE, p2_free)],
                                 inc=TE if dc == DC - 1 else None)
            if i == NSUB - 1:
                freew_blk[1] = vmm
                wload_next()
            vev = pr.add("vector", lambda e, i_=i: e.tensor_copy(v_sb[:, i_ * TT:(i_ + 1) * TT],
                                                                 p2[:, 0:TT]),
                         waits=[(TE, vmm)], inc=VE)
            p2_free = vev
            kv_ready = vev
            if DEBUG_EARLY and i == 0:
                for nm, apsrc, gate in (
                        ("e_xn", xn_sb[:], xn_done), ("e_kr", kr_sb[:, 0:TT], krope),
                        ("e_v", v_sb[:, 0:TT], vev)):
                    shp = [int(s) for s in apsrc.shape]
                    dap = nc.dram_tensor("dbg_" + nm, shp, F32, kind="ExternalOutput").ap()
                    pr.add("gpsimd",
                           (lambda d_, s_: (lambda e: e.dma_start(out=d_, in_=s_)))(dap, apsrc),
                           waits=[(VE, gate)], inc=DBG, inck=16)

        kblk = [2]

        def consume_block():
            k = kblk[0]
            kblk[0] += 1
            return k

        sig_cnt = [0, 0]
        asc_cnt = [0, 0]
        vmm_slot = [0, 0]
        c1_cnt = [0, 0]
        p3b_free = 0
        rope_cnt = 0
        qevac = p0_free
        aoev = 0
        yevac_c = [0, 0]
        yadd_c = [0, 0]
        ydma_c = [0, 0]
        hw_cnt = 0
        out_dmas = 0

        for t in range(NT_BUILD):
            sub = t + 1
            xc = load_x(sub, [(VE, x_free)])
            cc = load_cs(sub, [(VE, cs_free)])
            st["p3s_free"] = p3b_free
            xn_done, _ = emit_ln(xc, extra_bcast_waits=[(VE, aoev)] if aoev else [],
                                 p5_waits=[(VE, st["p3v_free"]), (VE, hw_cnt)])

            for h in range(DBG_HEADS):
                kq = consume_block()
                qmm = None
                for dc in range(DC):
                    qmm = pr.add("tensor",
                                 lambda e, d=dc, k_=kq: e.matmul(p0[:, 0:TT],
                                                                 wbufs[k_ % 3][:, d],
                                                                 xn_sb[:, d],
                                                                 start=(d == 0), stop=(d == DC - 1)),
                                 waits=[ldw_blk[kq], (VE, xn_done), (SE, qevac)],
                                 inc=TE if dc == DC - 1 else None)
                freew_blk[kq] = qmm
                wload_next()
                qevac = pr.add("scalar", lambda e: e.activation(out=qt1[:], in_=p0[:, 0:TT],
                                                                func=AF.Copy),
                               waits=[(TE, qmm), (VE, rope_cnt)], inc=SE)
                rot = pr.add("tensor", lambda e: e.matmul(p5[:, 0:TT], rm_sb[:], qt1[:],
                                                          start=True, stop=True),
                             waits=[(SE, qevac), (VE, hw_cnt), (VE, rope_cnt),
                                    (VE, st["p3v_free"])], inc=TE)
                pr.add("vector", lambda e: e.tensor_tensor(out=tmpA[:], in0=qt1[:], in1=cs_sb[:],
                                                           op=ALU.mult),
                       waits=[(SE, qevac), cc])
                pr.add("vector", lambda e: e.tensor_tensor(out=tmpB[:], in0=sn_sb[:],
                                                           in1=p5[:, 0:TT], op=ALU.mult),
                       waits=[(TE, rot)])
                rope_cnt = pr.add("vector", lambda e: e.tensor_tensor(out=qr_sb[:], in0=tmpA[:],
                                                                      in1=tmpB[:], op=ALU.add),
                                  inc=VE)

                for qc in range(2):
                    for w in range(2):
                        ss = (qc * 2 + w) % 2
                        koff = t * TT + w * TT
                        cidx = t * 4 + qc * 2 + w
                        smm = pr.add("tensor",
                                     lambda e, q_=qc, ko=koff, s_=ss: e.matmul(
                                         pS[s_][:, 0:512],
                                         qr_sb[:, q_ * 128:(q_ + 1) * 128],
                                         kr_sb[:, ko:ko + 512],
                                         start=True, stop=True),
                                     waits=[(VE, rope_cnt), (VE, kv_ready), (SE, sig_cnt[ss])],
                                     inc=TE)
                        pr.add("scalar",
                               lambda e, s_=ss: e.activation(out=an[s_][:],
                                                             in_=pS[s_][:, 0:512],
                                                             func=AF.Sigmoid, scale=SCALE,
                                                             accum_out=scol[:, s_:s_ + 1]),
                               waits=[(TE, smm), (VE, asc_cnt[ss]), (VE, c1_cnt[ss])])
                        sgc = pr.add("scalar",
                                     lambda e, s_=ss: e.activation(out=sdum[:],
                                                                   in_=scol[:, s_:s_ + 1],
                                                                   func=AF.Copy),
                                     inc=SE)
                        sig_cnt[ss] = sgc
                        c1_cnt[ss] = pr.add("vector",
                                            lambda e, s_=ss, ci=cidx: e.tensor_scalar(
                                                out=ctmp[:], in0=scol[:, s_:s_ + 1],
                                                scalar1=su_sb[:, ci:ci + 1], scalar2=None,
                                                op0=ALU.subtract),
                                            waits=[(SE, sgc), ldw_c["su"]], inc=VE)
                        pr.add("vector", lambda e: e.reciprocal(ctmp2[:], ctmp[:]))
                        ccd = pr.add("vector",
                                     lambda e, ci=cidx: e.tensor_tensor(out=ccol[:], in0=ctmp2[:],
                                                                        in1=cf_sb[:, ci:ci + 1],
                                                                        op=ALU.mult),
                                     waits=[ldw_c["cf"]], inc=VE)
                        ac = pr.add("vector",
                                    lambda e, s_=ss: e.tensor_scalar(out=asc[s_][:], in0=an[s_][:],
                                                                     scalar1=ccol[:], scalar2=None,
                                                                     op0=ALU.mult),
                                    waits=[(VE, ccd)], inc=VE)
                        asc_cnt[ss] = ac
                        trc = None
                        for j in range(4):
                            trc = pr.add("tensor",
                                         lambda e, j_=j, s_=ss: e.transpose(
                                             p3[:, j_ * 128:(j_ + 1) * 128],
                                             asc[s_][:, j_ * 128:(j_ + 1) * 128],
                                             im_sb[:]),
                                         waits=[(VE, ac), (SE, p3b_free), ldw_c["im"]],
                                         inc=TE if j == 3 else None)
                        aTe = pr.add("scalar",
                                     lambda e, s_=ss: e.activation(out=aT[s_][:], in_=p3[:, 0:512],
                                                                   func=AF.Copy),
                                     waits=[(TE, trc), (TE, vmm_slot[ss])], inc=SE)
                        p3b_free = aTe
                        vmm = None
                        for j in range(4):
                            tc = koff // 128 + j
                            vmm = pr.add("tensor",
                                         lambda e, j_=j, t_=tc, q_=qc, s_=ss, w_=w: e.matmul(
                                             p4[:, q_ * 128:(q_ + 1) * 128],
                                             v_sb[:, t_ * 128:(t_ + 1) * 128],
                                             aT[s_][:, j_ * 128:(j_ + 1) * 128],
                                             start=(w_ == 0 and j_ == 0),
                                             stop=(w_ == 1 and j_ == 3)),
                                         waits=[(SE, aTe), (VE, aoev)],
                                         inc=TE if (w == 1 and j == 3) else None)
                        if vmm is not None:
                            vmm_slot[ss] = vmm
                aoev = pr.add("vector", lambda e, h_=h: e.tensor_copy(ao_sb[:, h_], p4[:, 0:TT]),
                              waits=[(TE, vmm_slot[1])], inc=VE)
            if not DBG_FFN:
                x_free = xn_done
                continue
            lgmm = None
            for dc in range(DC):
                lgmm = pr.add("tensor",
                              lambda e, d=dc: e.matmul(p5[0:8, 0:TT], wr_sb[:, d], xn_sb[:, d],
                                                       start=(d == 0), stop=(d == DC - 1)),
                              waits=[ldw_c["wr"], (VE, xn_done), (VE, hw_cnt),
                                     (VE, rope_cnt)],
                              inc=TE if dc == DC - 1 else None)
            lgev = pr.add("scalar", lambda e: e.activation(out=lg_sb[:], in_=p5[0:8, 0:TT],
                                                           func=AF.Copy),
                          waits=[(TE, lgmm)], inc=SE)
            wtd = 0
            wtr = None
            for qc in range(2):
                ltr = pr.add("tensor",
                             lambda e, q_=qc: e.transpose(p3[:, q_ * 8:(q_ + 1) * 8],
                                                          lg_sb[0:8, q_ * 128:(q_ + 1) * 128],
                                                          im_sb[0:8, 0:8]),
                             waits=[(SE, lgev), (SE, p3b_free), (VE, wtd)], inc=TE)
                lgt = lgt2[qc]
                cpd = pr.add("vector", lambda e, q_=qc, g_=lgt: e.tensor_copy(
                    g_[:], p3[:, q_ * 8:(q_ + 1) * 8]),
                       waits=[(TE, ltr)], inc=VE)
                m1d = pr.add("vector", lambda e, g_=lgt: e.reduce_max(m1c[:], g_[:], axis=AX.X),
                             waits=[(VE, cpd)], inc=VE)
                mk1d = pr.add("vector", lambda e, g_=lgt: e.tensor_scalar(
                    out=mk1[:], in0=g_[:], scalar1=m1c[:], scalar2=None, op0=ALU.is_equal),
                       waits=[(VE, m1d)], inc=VE)
                tbd = pr.add("vector", lambda e: e.tensor_scalar(out=tb8[:], in0=mk1[:],
                                                                 scalar1=-1e30,
                                                                 scalar2=None, op0=ALU.mult),
                             waits=[(VE, mk1d)], inc=VE)
                l2d = pr.add("vector", lambda e, g_=lgt: e.tensor_tensor(out=l2[:], in0=tb8[:],
                                                                         in1=g_[:], op=ALU.add),
                             waits=[(VE, tbd)], inc=VE)
                m2d = pr.add("vector", lambda e: e.reduce_max(m2c[:], l2[:], axis=AX.X),
                             waits=[(VE, l2d)], inc=VE)
                dcd = pr.add("vector", lambda e: e.tensor_tensor(out=dcol[:], in0=m1c[:],
                                                                 in1=m2c[:], op=ALU.subtract),
                             waits=[(VE, m2d)], inc=VE)
                mkd = pr.add("vector", lambda e: e.tensor_scalar(out=mk2[:], in0=l2[:],
                                                                 scalar1=m2c[:], scalar2=None,
                                                                 op0=ALU.is_equal),
                             waits=[(VE, dcd)], inc=VE)
                ppd = pr.add("scalar",
                             lambda e: (e.activation(out=p1c[:], in_=dcol[:], func=AF.Sigmoid),
                                        e.activation(out=p2c[:], in_=dcol[:], func=AF.Sigmoid,
                                                     scale=-1.0))[-1],
                             waits=[(VE, mkd)], inc=SE)
                w1d = pr.add("vector", lambda e: e.tensor_scalar(out=w1c[:], in0=mk1[:],
                                                                 scalar1=p1c[:],
                                                                 scalar2=None, op0=ALU.mult),
                             waits=[(SE, ppd)], inc=VE)
                w2d = pr.add("vector", lambda e: e.tensor_scalar(out=w2c[:], in0=mk2[:],
                                                                 scalar1=p2c[:],
                                                                 scalar2=None, op0=ALU.mult),
                             waits=[(VE, w1d)], inc=VE)
                wtd = pr.add("vector", lambda e: e.tensor_tensor(out=wtok[:], in0=w1c[:],
                                                                 in1=w2c[:], op=ALU.add),
                             waits=[(VE, w2d)], inc=VE)
                wtr = pr.add("tensor",
                             lambda e, q_=qc: e.transpose(
                                 p3[0:8, 128 + q_ * 128:128 + (q_ + 1) * 128],
                                 wtok[:], im_sb[:]),
                             waits=[(VE, wtd)], inc=TE)
            wfev = pr.add("scalar", lambda e: e.activation(out=wfm[:], in_=p3[0:8, 128:128 + TT],
                                                           func=AF.Copy),
                          waits=[(TE, wtr)], inc=SE)
            p3b_free = wfev
            wxm = pr.add("tensor", lambda e: e.matmul(p5[:, 0:TT], em_sb[:], wfm[:],
                                                      start=True, stop=True),
                         waits=[(SE, wfev), ldw_c["em"]], inc=TE)
            wexp_done = pr.add("vector", lambda e: e.tensor_copy(wexp_sb[:], p5[:, 0:TT]),
                               waits=[(TE, wxm)], inc=VE)
            kat = consume_block()
            hmm_c = None
            for dc in range(DC):
                hmm_c = pr.add("tensor",
                               lambda e, d=dc, k_=kat: e.matmul(p5[:, TT:2 * TT],
                                                                wbufs[k_ % 3][:, d],
                                                                xn_sb[:, d],
                                                                start=(d == 0), stop=(d == DC - 1)),
                               waits=[ldw_blk[kat], (VE, wexp_done)],
                               inc=TE if dc == DC - 1 else None)
            freew_blk[kat] = hmm_c
            wload_next()
            hw_cnt = pr.add("vector", lambda e: e.tensor_tensor(out=hw_sb[:], in0=p5[:, TT:2 * TT],
                                                                in1=wexp_sb[:], op=ALU.mult),
                            waits=[(TE, hmm_c)], inc=VE)

            if not DBG_Y:
                x_free = xn_done
                continue
            yadd_last = None
            for oc in range(DC):
                kwo = consume_block()
                kwb = consume_block()
                pp = oc % 2
                pr.add("tensor",
                       lambda e, k_=kwo, p_=pp: e.matmul(pY[p_][:, 0:TT],
                                                         wbufs[k_ % 3][:, DC], hw_sb[:],
                                                         start=True, stop=False),
                       waits=[ldw_blk[kwo], (VE, hw_cnt), (SE, yevac_c[pp])])
                wo_last = None
                for dc in range(DC):
                    wo_last = pr.add("tensor",
                                     lambda e, d=dc, k_=kwo, p_=pp: e.matmul(
                                         pY[p_][:, 0:TT], wbufs[k_ % 3][:, d], ao_sb[:, d],
                                         start=False, stop=False),
                                     waits=[(VE, aoev)],
                                     inc=TE if dc == DC - 1 else None)
                freew_blk[kwo] = wo_last
                wload_next()
                wb_last = None
                for dc in range(DC):
                    wb_last = pr.add("tensor",
                                     lambda e, d=dc, k_=kwb, p_=pp: e.matmul(
                                         pY[p_][:, 0:TT], wbufs[k_ % 3][:, d], xn_sb[:, d],
                                         start=False, stop=(d == DC - 1)),
                                     waits=[ldw_blk[kwb]],
                                     inc=TE if dc == DC - 1 else None)
                ymm = wb_last
                freew_blk[kwb] = ymm
                wload_next()
                yev = pr.add("scalar",
                             lambda e, o_=oc, p_=pp: e.activation(out=ya[p_][:],
                                                                  in_=pY[p_][:, 0:TT],
                                                                  func=AF.Identity,
                                                                  bias=bb_sb[:, o_:o_ + 1]),
                             waits=[(TE, ymm), (VE, yadd_c[pp]), ldw_c["bb"]], inc=SE)
                yevac_c[pp] = yev
                yad = pr.add("vector",
                             lambda e, o_=oc, p_=pp: e.tensor_tensor(out=yb[p_][:], in0=ya[p_][:],
                                                                     in1=x_sb[:, o_], op=ALU.add),
                             waits=[(SE, yev), (STOs[pp], ydma_c[pp])], inc=VE)
                yadd_c[pp] = yad
                yadd_last = yad
                ydma_c[pp] = pr.add("gpsimd",
                                    lambda e, t_=t, o_=oc, p_=pp: e.dma_start(out=y_out[t_, o_],
                                                                              in_=yb[p_][:]),
                                    waits=[(VE, yad)], inc=STOs[pp], inck=16)
                out_dmas += 1
            x_free = yadd_last

        if DEBUG_DUMP:
            dbg_bufs = dict(xn=xn_sb, kr=kr_sb, v=v_sb, ao=ao_sb, x=x_sb,
                            can1=can1, can2=can2, can3=can3,
                            an0=an[0], an1=an[1], asc0=asc[0], asc1=asc[1],
                            aT0=aT[0], aT1=aT[1], scol=scol, ccol=ccol,
                            lgt0=lgt2[0], lgt1=lgt2[1], mk1=mk1, mk2=mk2, l2=l2,
                            rr=rr_t, m2r=m2_t, ra=ra, rb=rb,
                            qr=qr_sb, hw=hw_sb, wexp=wexp_sb, lg=lg_sb,
                            wtok=wtok, m1c=m1c, m2c=m2c, p1c=p1c, p2c=p2c)
            endw = [(TE, TE.n), (SE, SE.n), (VE, VE.n)]
            for nm, buf in dbg_bufs.items():
                shp = [int(s) for s in buf[:].shape]
                dap = nc.dram_tensor("dbg_" + nm, shp, F32, kind="ExternalOutput").ap()
                pr.add("gpsimd",
                       (lambda d_, b_: (lambda e: e.dma_start(out=d_, in_=b_[:])))(dap, buf),
                       waits=endw, inc=DBG, inck=16)
        fin = [(STOs[0], STOs[0].n), (STOs[1], STOs[1].n)]
        if DBG.n:
            fin.append((DBG, DBG.n))
        pr.add("gpsimd", None, waits=fin)

        block = ctx.enter_context(nc.Block())

        @block.sync
        def _(e):
            pr.emit("sync", e)

        @block.tensor
        def _(e):
            pr.emit("tensor", e)

        @block.scalar
        def _(e):
            pr.emit("scalar", e)

        @block.vector
        def _(e):
            pr.emit("vector", e)

        @block.gpsimd
        def _(e):
            pr.emit("gpsimd", e)

    return nc



def _rope_tables():
    base = 10000.0
    inv_freq = 1.0 / base ** (np.arange(0, DH, 2, dtype=np.float32) / DH)
    t = np.arange(S, dtype=np.float32)
    f = np.outer(t, inv_freq)
    cos = np.zeros((S, DH), dtype=np.float32)
    sin = np.zeros((S, DH), dtype=np.float32)
    cos[:, 0::2] = np.cos(f)
    cos[:, 1::2] = np.cos(f)
    sin[:, 0::2] = np.sin(f)
    sin[:, 1::2] = np.sin(f)
    return cos.T.copy(), sin.T.copy()


def _pack_wt(wt):
    m = wt.shape[1]
    return np.ascontiguousarray(wt.reshape(DC, 128, m).transpose(1, 0, 2))


def prep_in_maps(L0_attn_x, L0_attn_Wq, L0_attn_Wk, L0_attn_Wv, L0_attn_Wo,
                 L0_ffn_W_base, L0_ffn_b_base, L0_ffn_W_router, L0_ffn_A, L0_ffn_B):
    f32 = np.float32
    x = np.asarray(L0_attn_x, dtype=f32)
    Wq = np.asarray(L0_attn_Wq, dtype=f32)
    Wk = np.asarray(L0_attn_Wk, dtype=f32)
    Wv = np.asarray(L0_attn_Wv, dtype=f32)
    Wo = np.asarray(L0_attn_Wo, dtype=f32)
    Wb = np.asarray(L0_ffn_W_base, dtype=f32)
    bb = np.asarray(L0_ffn_b_base, dtype=f32)
    Wr = np.asarray(L0_ffn_W_router, dtype=f32)
    A = np.asarray(L0_ffn_A, dtype=f32)
    Bm = np.asarray(L0_ffn_B, dtype=f32)

    wq_p = np.ascontiguousarray(
        Wq.T.reshape(DC, 128, DC, 128).transpose(2, 1, 0, 3))
    wo_p4 = Wo.T.reshape(DC, 128, DC, 128).transpose(2, 1, 0, 3)
    bfT = Bm.transpose(0, 2, 1).reshape(128, D)
    wo_p = np.zeros((DC, 128, DC + 1, 128), dtype=f32)
    wo_p[:, :, :DC, :] = wo_p4
    for oc in range(DC):
        wo_p[oc, :, DC, :] = bfT[:, oc * 128:(oc + 1) * 128]
    wo_p = np.ascontiguousarray(wo_p)
    wb_p = np.ascontiguousarray(
        Wb.T.reshape(DC, 128, DC, 128).transpose(2, 1, 0, 3))
    wk_p = _pack_wt(np.ascontiguousarray(Wk.T))
    wv_p = _pack_wt(np.ascontiguousarray(Wv.T))
    at_p = _pack_wt(np.ascontiguousarray(A.reshape(128, D).T))
    wr_p = _pack_wt(np.ascontiguousarray(Wr.T))
    bb_p = np.ascontiguousarray(bb.reshape(DC, 128).T)

    rm = np.zeros((128, 128), dtype=f32)
    half = DH // 2
    for j in range(half):
        rm[half + j, j] = -1.0
        rm[j, half + j] = 1.0
    im = np.eye(128, dtype=f32)
    em = np.zeros((8, 128), dtype=f32)
    for e_ in range(8):
        em[e_, e_ * 16:(e_ + 1) * 16] = 2.0
    ones1 = np.ones((1, 128), dtype=f32)
    onesc = np.ones((128, 1), dtype=f32)

    cosT, sinT = _rope_tables()
    alpha64 = np.linspace(0.0, 1.0, 256)

    in_maps = []
    for c in range(NCORES):
        b = c // 4
        q0 = (c % 4) * TOK
        lo = q0 - HALO
        arr = np.zeros((AR, D), dtype=f32)
        s0 = max(lo, 0)
        s1 = min(lo + AR, S)
        arr[s0 - lo:s1 - lo] = x[b, s0:s1]
        xs_p = np.ascontiguousarray(
            arr.reshape(NSUB, TT, DC, 128).transpose(0, 3, 2, 1))

        cos_p = np.zeros((NSUB, 128, TT), dtype=f32)
        sin_p = np.zeros((NSUB, 128, TT), dtype=f32)
        cslc = np.zeros((128, AR), dtype=f32)
        snlc = np.zeros((128, AR), dtype=f32)
        cslc[:, s0 - lo:s1 - lo] = cosT[:, s0:s1]
        snlc[:, s0 - lo:s1 - lo] = sinT[:, s0:s1]
        for i in range(NSUB):
            cos_p[i] = cslc[:, i * TT:(i + 1) * TT]
            sin_p[i] = snlc[:, i * TT:(i + 1) * TT]

        seq_start = (c % 4 == 0)
        seq_end = (c % 4 == 3)
        coef = np.zeros((128, NT * 4), dtype=f32)
        subc = np.zeros((128, NT * 4), dtype=f32)
        j = np.arange(128)
        for t in range(NT):
            for qc in range(2):
                aa = alpha64[qc * 128 + j]
                idx = t * 4 + qc * 2
                if seq_start and t == 0:
                    coef[:, idx] = 0.0
                    coef[:, idx + 1] = 1.0
                else:
                    coef[:, idx] = (1.0 - aa).astype(f32)
                    coef[:, idx + 1] = aa.astype(f32)
                if seq_end and t == NT - 1:
                    subc[:, idx + 1] = 128.0
        in_maps.append(dict(
            xs_p=xs_p, cos_p=cos_p, sin_p=sin_p,
            wk_p=wk_p, wv_p=wv_p, wq_p=wq_p, wo_p=wo_p, wb_p=wb_p,
            at_p=at_p, wr_p=wr_p, bb_p=bb_p,
            coef_p=coef, sub_p=subc,
            rm_p=rm, im_p=im, em_p=em, ones1_p=ones1, onesc_p=onesc,
            eps_p=np.full((1, 1), EPS, dtype=f32),
            can_p=np.full((128, 256), 777.0, dtype=f32),
        ))
    return in_maps


def kernel(**inputs):
    in_maps = prep_in_maps(**inputs)
    nc = build_nc()
    res = run_bass_kernel_spmd(nc, in_maps, core_ids=list(range(NCORES)))
    out = np.empty((NCORES * TOK, D), dtype=np.float32)
    for c in range(NCORES):
        a = np.asarray(res.results[c]["y_out"])
        out[c * TOK:(c + 1) * TOK] = a.transpose(0, 3, 1, 2).reshape(TOK, D)
    return out.reshape(B, S, D)


# version 03
# speedup vs baseline: 1.4117x; 1.4117x over previous
import contextlib
import numpy as np
import concourse.bass as bass
import concourse.mybir as mybir
from concourse.bass_utils import run_bass_kernel_spmd

F32 = mybir.dt.float32
AF = mybir.ActivationFunctionType
ALU = mybir.AluOpType
AX = mybir.AxisListType

D = 4096
DC = 32
H = 32
DH = 128
B = 2
S = 8192
NCORES = 8
TOK = 2048
TT = 256
NT = TOK // TT
HALO = 256
AR = TOK + 2 * HALO
NSUB = AR // TT
SCALE = 1.0 / float(np.sqrt(128.0))
EPS = 1e-5
INV_D = 1.0 / float(D)

NT_BUILD = NT
DEBUG_DUMP = False
DBG_HEADS = H
DEBUG_EARLY = False
DBG_FFN = True
DBG_Y = True


class Sem:
    __slots__ = ("h", "n")

    def __init__(self, h):
        self.h = h
        self.n = 0


class Prog:
    ENGINES = ("tensor", "scalar", "vector", "gpsimd", "sync")

    def __init__(self):
        self.ops = {e: [] for e in self.ENGINES}
        self.lastwait = {e: {} for e in self.ENGINES}
        self.chain_sem = {}

    def add(self, eng, fn, waits=(), inc=None, inck=1):
        cs = self.chain_sem.get(eng)
        waits = list(waits)
        if cs is not None and fn is not None:
            waits.append((cs, cs.n))
            if inc is None:
                inc = cs
        lw = self.lastwait[eng]
        ws = []
        for sem, c in waits:
            if c is None or c <= 0:
                continue
            assert c <= sem.n, f"wait on future count {c} > {sem.n} ({eng})"
            if lw.get(id(sem), 0) < c:
                ws.append((sem, c))
                lw[id(sem)] = c
        self.ops[eng].append((ws, fn, inc, inck))
        if inc is not None:
            inc.n += inck
            return inc.n
        return None

    def emit(self, name, eng):
        for ws, fn, inc, k in self.ops[name]:
            for sem, c in ws:
                eng.wait_ge(sem.h, c)
            ins = fn(eng) if fn is not None else None
            if inc is not None:
                if ins is None:
                    ins = eng.nop()
                ins.then_inc(inc.h, k)


def build_nc():
    nc = bass.Bass("TRN2", target_bir_lowering=False)

    def din(name, shape):
        return nc.dram_tensor(name, list(shape), F32, kind="ExternalInput").ap()

    xs_p = din("xs_p", (NSUB, 128, DC, TT))
    cos_p = din("cos_p", (NSUB, 128, TT))
    sin_p = din("sin_p", (NSUB, 128, TT))
    wk_p = din("wk_p", (128, DC, 128))
    wv_p = din("wv_p", (128, DC, 128))
    wq_p = din("wq_p", (H, 128, DC, 128))
    wo_p = din("wo_p", (DC, 128, DC + 1, 128))
    wb_p = din("wb_p", (DC, 128, DC, 128))
    at_p = din("at_p", (128, DC, 128))
    wr_p = din("wr_p", (128, DC, 8))
    bb_p = din("bb_p", (128, DC))
    coef_p = din("coef_p", (128, NT * 4))
    sub_p = din("sub_p", (128, NT * 4))
    rm_p = din("rm_p", (128, 128))
    im_p = din("im_p", (128, 128))
    em_p = din("em_p", (8, 128))
    ones1_p = din("ones1_p", (1, 128))
    onesc_p = din("onesc_p", (128, 1))
    eps_p = din("eps_p", (1, 1))
    can_p = din("can_p", (128, 256))

    y_out = nc.dram_tensor("y_out", [NT, DC, 128, TT], F32, kind="ExternalOutput").ap()

    with contextlib.ExitStack() as ctx:
        def sb(name, shape):
            return ctx.enter_context(nc.sbuf_tensor(name, list(shape), F32))

        def psum(name):
            return ctx.enter_context(nc.psum_tensor(name, [128, 512], F32))

        def sem(name):
            return Sem(ctx.enter_context(nc.semaphore(name)))

        x_sb = sb("x_sb", (128, DC, TT))
        xn_sb = sb("xn_sb", (128, DC, TT))
        ao_sb = sb("ao_sb", (128, H, TT))
        can1 = sb("can1", (128, 256))
        kr_sb = sb("kr_sb", (128, AR))
        can2 = sb("can2", (128, 256))
        v_sb = sb("v_sb", (128, AR))
        can3 = sb("can3", (128, 256))
        wbufs = [sb(f"wbuf{i}", (128, DC + 1, 128)) for i in range(3)]
        cs_sb = sb("cs_sb", (128, TT))
        sn_sb = sb("sn_sb", (128, TT))
        qt1 = sb("qt1", (128, TT))
        tmpA = sb("tmpA", (128, TT))
        tmpB = sb("tmpB", (128, TT))
        qr_sb = sb("qr_sb", (128, TT))
        sq_sb = sb("sq_sb", (128, TT))
        an = [sb(f"an{i}", (128, 512)) for i in range(2)]
        asc = [sb(f"asc{i}", (128, 512)) for i in range(2)]
        aT = [sb(f"aT{i}", (128, 512)) for i in range(2)]
        scol = sb("scol", (128, 2))
        sdum = sb("sdum", (128, 1))
        ctmp = sb("ctmp", (128, 1))
        ctmp2 = sb("ctmp2", (128, 1))
        ccol = sb("ccol", (128, 1))
        ra = sb("ra", (1, TT))
        rb = sb("rb", (1, TT))
        rc = sb("rc", (1, TT))
        rd = sb("rd", (1, TT))
        rn = sb("rn", (1, TT))
        ro = sb("ro", (1, TT))
        rr_t = sb("rr_t", (1, TT))
        m2_t = sb("m2_t", (1, TT))
        lg_sb = sb("lg_sb", (8, TT))
        lgt2 = [sb("lgt0", (128, 8)), sb("lgt1", (128, 8))]
        mk1 = sb("mk1", (128, 8))
        tb8 = sb("tb8", (128, 8))
        l2 = sb("l2", (128, 8))
        mk2 = sb("mk2", (128, 8))
        m1c = sb("m1c", (128, 1))
        m2c = sb("m2c", (128, 1))
        dcol = sb("dcol", (128, 1))
        p1c = sb("p1c", (128, 1))
        p2c = sb("p2c", (128, 1))
        w1c = sb("w1c", (128, 8))
        w2c = sb("w2c", (128, 8))
        wtok = sb("wtok", (128, 8))
        wfm = sb("wfm", (8, TT))
        wexp_sb = sb("wexp_sb", (128, TT))
        hw_sb = sb("hw_sb", (128, TT))
        ya = [sb(f"ya{i}", (128, TT)) for i in range(2)]
        yb = [sb(f"yb{i}", (128, TT)) for i in range(2)]
        wr_sb = sb("wr_sb", (128, DC, 8))
        bb_sb = sb("bb_sb", (128, DC))
        cf_sb = sb("cf_sb", (128, NT * 4))
        su_sb = sb("su_sb", (128, NT * 4))
        rm_sb = sb("rm_sb", (128, 128))
        im_sb = sb("im_sb", (128, 128))
        em_sb = sb("em_sb", (8, 128))
        o1_sb = sb("o1_sb", (1, 128))
        oc_sb = sb("oc_sb", (128, 1))
        eps_sb = sb("eps_sb", (1, 1))

        p0 = psum("p0")
        p1 = psum("p1")
        p2 = psum("p2")
        p3 = psum("p3")
        p4 = psum("p4")
        p5 = psum("p5")
        p6 = psum("p6")
        p7 = psum("p7")
        pS = [p1, p2]
        pY = [p6, p7]

        TE = sem("TE")
        SE = sem("SE")
        VE = sem("VE")
        LDC = sem("LDC")
        LDWs = [sem(f"LDW{i}") for i in range(3)]
        LDXX = sem("LDXX")
        LDXC = sem("LDXC")
        STOs = [sem("STO0"), sem("STO1")]
        DBG = sem("DBG")

        pr = Prog()
        pr.chain_sem["vector"] = VE
        pr.chain_sem["scalar"] = SE

        ldw_c = {}
        for nm, dst, src in [
            ("wr", wr_sb, wr_p), ("bb", bb_sb, bb_p),
            ("cf", cf_sb, coef_p), ("su", su_sb, sub_p), ("rm", rm_sb, rm_p),
            ("im", im_sb, im_p), ("em", em_sb, em_p), ("o1", o1_sb, ones1_p),
            ("oc", oc_sb, onesc_p), ("eps", eps_sb, eps_p),
            ("c1", can1, can_p), ("c2", can2, can_p), ("c3", can3, can_p),
        ]:
            pr.add("sync",
                   (lambda d, s: (lambda e: e.dma_start(out=d[:], in_=s)))(dst, src),
                   inc=LDC, inck=16)
        for nm in list(ldw_c) or []:
            pass
        _all_c = LDC.n
        for nm in ("wr", "bb", "cf", "su", "rm", "im", "em", "o1", "oc", "eps",
                   "c1", "c2", "c3"):
            ldw_c[nm] = None
        ldw_c = {k: (LDC, _all_c) for k in ldw_c}

        wstream = [(wk_p, DC), (wv_p, DC)]
        for _t in range(NT_BUILD):
            wstream.extend((wq_p[h], DC) for h in range(DBG_HEADS))
            if DBG_FFN:
                wstream.append((at_p, DC))
            if DBG_Y:
                for oc in range(DC):
                    wstream.append((wo_p[oc], DC + 1))
                    wstream.append((wb_p[oc], DC))
        ldw_blk = [None] * len(wstream)
        freew_blk = [None] * len(wstream)
        next_load = [0]

        def wload_next():
            k = next_load[0]
            if k >= len(wstream):
                return
            next_load[0] += 1
            waits = []
            if k >= 3:
                assert freew_blk[k - 3] is not None, f"block {k-3} not yet freed"
                waits.append((TE, freew_blk[k - 3]))
            src, ncol = wstream[k]
            buf = wbufs[k % 3]
            s_sem = LDWs[k % 3]
            dst = buf[:] if ncol == DC + 1 else buf[:, 0:DC]
            cnt = pr.add("sync",
                         (lambda d_, s_: (lambda e: e.dma_start(out=d_, in_=s_)))(dst, src),
                         waits=waits, inc=s_sem, inck=16)
            ldw_blk[k] = (s_sem, cnt)

        wload_next(); wload_next(); wload_next()

        def load_x(sub, waits):
            c = pr.add("gpsimd", lambda e, s=sub: e.dma_start(out=x_sb[:], in_=xs_p[s]),
                       waits=waits, inc=LDXX, inck=16)
            return (LDXX, c)

        def load_cs(sub, waits):
            pr.add("gpsimd", lambda e, s=sub: e.dma_start(out=cs_sb[:], in_=cos_p[s]),
                   waits=waits, inc=LDXC, inck=16)
            c2 = pr.add("gpsimd", lambda e, s=sub: e.dma_start(out=sn_sb[:], in_=sin_p[s]),
                        inc=LDXC, inck=16)
            return (LDXC, c2)

        st = dict(sq_free=0, p3v_free=0, p3s_free=0, p4_free=0, rows_free=0)

        def emit_ln(xc, extra_bcast_waits=(), p5_waits=()):
            mm_sq = None
            for dc in range(DC):
                sqw = [xc]
                if st["sq_free"]:
                    sqw.append((TE, st["sq_free"]))
                sq_c = pr.add("scalar",
                              lambda e, d=dc: e.activation(out=sq_sb[:], in_=x_sb[:, d],
                                                           func=AF.Square),
                              waits=sqw, inc=SE)
                pr.add("tensor",
                       lambda e, d=dc: e.matmul(p3[0:1, 0:TT], oc_sb[:], x_sb[:, d],
                                                start=(d == 0), stop=(d == DC - 1)),
                       waits=[xc, ldw_c["oc"],
                              (VE, st["p3v_free"]), (SE, st["p3s_free"])])
                mm_sq = pr.add("tensor",
                               lambda e, d=dc: e.matmul(p5[0:1, 0:TT], oc_sb[:], sq_sb[:],
                                                        start=(d == 0), stop=(d == DC - 1)),
                               waits=[(SE, sq_c)] + (list(p5_waits) if dc == 0 else []),
                               inc=TE)
                st["sq_free"] = mm_sq
            pr.add("vector", lambda e: e.tensor_scalar(out=ra[:], in0=p3[0:1, 0:TT],
                                                       scalar1=INV_D, scalar2=None, op0=ALU.mult),
                   waits=[(TE, mm_sq), (TE, st["rows_free"])])
            v2 = pr.add("vector", lambda e: e.tensor_scalar(out=rb[:], in0=p5[0:1, 0:TT],
                                                            scalar1=INV_D, scalar2=None,
                                                            op0=ALU.mult),
                        inc=VE)
            st["p3v_free"] = v2
            pr.add("vector", lambda e: e.tensor_tensor(out=rc[:], in0=ra[:], in1=ra[:], op=ALU.mult))
            v4 = pr.add("vector", lambda e: e.tensor_tensor(out=rd[:], in0=rb[:], in1=rc[:],
                                                            op=ALU.subtract),
                        inc=VE)
            sqr = pr.add("scalar", lambda e: e.activation(out=rc[:], in_=rd[:], func=AF.Sqrt,
                                                          bias=eps_sb[:]),
                         waits=[(VE, v4), ldw_c["eps"]], inc=SE)
            pr.add("vector", lambda e: e.tensor_scalar(out=rb[:], in0=rd[:], scalar1=EPS,
                                                       scalar2=None, op0=ALU.add),
                   waits=[(SE, sqr)])
            pr.add("vector", lambda e: e.reciprocal(rn[:], rc[:]))
            pr.add("vector", lambda e: e.tensor_tensor(out=rc[:], in0=rn[:], in1=rn[:], op=ALU.mult))
            pr.add("vector", lambda e: e.tensor_tensor(out=rd[:], in0=rc[:], in1=rb[:], op=ALU.mult))
            pr.add("vector", lambda e: e.tensor_scalar(out=rc[:], in0=rd[:], scalar1=-0.5,
                                                       scalar2=1.5, op0=ALU.mult, op1=ALU.add))
            pr.add("vector", lambda e: e.tensor_tensor(out=ro[:], in0=rn[:], in1=rc[:], op=ALU.mult))
            pr.add("vector", lambda e: e.tensor_tensor(out=rc[:], in0=ro[:], in1=ro[:], op=ALU.mult))
            pr.add("vector", lambda e: e.tensor_tensor(out=rd[:], in0=rc[:], in1=rb[:], op=ALU.mult))
            pr.add("vector", lambda e: e.tensor_scalar(out=rc[:], in0=rd[:], scalar1=-0.5,
                                                       scalar2=1.5, op0=ALU.mult, op1=ALU.add))
            pr.add("vector", lambda e: e.tensor_tensor(out=rr_t[:], in0=ro[:], in1=rc[:], op=ALU.mult))
            rows_done = pr.add("vector", lambda e: e.tensor_tensor(out=m2_t[:], in0=ra[:],
                                                                   in1=rr_t[:], op=ALU.mult),
                               inc=VE)
            pr.add("tensor", lambda e: e.matmul(p4[:, 0:TT], o1_sb[:], rr_t[:],
                                                start=True, stop=True),
                   waits=[(VE, rows_done), ldw_c["o1"],
                          (VE, st["p4_free"])] + list(extra_bcast_waits))
            bcast_done = pr.add("tensor", lambda e: e.matmul(p4[:, TT:2 * TT], o1_sb[:], m2_t[:],
                                                             start=True, stop=True),
                                inc=TE)
            st["rows_free"] = bcast_done
            xn_done = None
            for dc in range(DC):
                pr.add("vector", lambda e, d=dc: e.tensor_tensor(out=tmpA[:], in0=x_sb[:, d],
                                                                 in1=p4[:, 0:TT], op=ALU.mult),
                       waits=[(TE, bcast_done), xc])
                xn_done = pr.add("vector",
                                 lambda e, d=dc: e.tensor_tensor(out=xn_sb[:, d], in0=tmpA[:],
                                                                 in1=p4[:, TT:2 * TT],
                                                                 op=ALU.subtract),
                                 inc=VE)
            st["p4_free"] = xn_done
            return xn_done, bcast_done

        x_free = 0
        cs_free = 0
        p0_free = 0
        p1_free = 0
        p2_free = 0
        kv_ready = 0

        for i in range(NSUB):
            xc = load_x(i, [(VE, x_free)])
            cc = load_cs(i, [(VE, cs_free)])
            xn_done, _ = emit_ln(xc, p5_waits=[(VE, st["p3v_free"])])
            x_free = xn_done

            kmm = None
            for dc in range(DC):
                kmm = pr.add("tensor",
                             lambda e, d=dc: e.matmul(p0[:, 0:TT], wbufs[0][:, d], xn_sb[:, d],
                                                      start=(d == 0), stop=(d == DC - 1)),
                             waits=[ldw_blk[0], (VE, xn_done), (SE, p0_free)],
                             inc=TE if dc == DC - 1 else None)
            if i == NSUB - 1:
                freew_blk[0] = kmm
                wload_next()
            kev = pr.add("scalar", lambda e: e.activation(out=qt1[:], in_=p0[:, 0:TT], func=AF.Copy),
                         waits=[(TE, kmm), (VE, cs_free)], inc=SE)
            p0_free = kev
            rot = pr.add("tensor", lambda e: e.matmul(p1[:, 0:TT], rm_sb[:], qt1[:],
                                                      start=True, stop=True),
                         waits=[(SE, kev), ldw_c["rm"], (VE, p1_free)], inc=TE)
            pr.add("vector", lambda e: e.tensor_tensor(out=tmpA[:], in0=qt1[:], in1=cs_sb[:],
                                                       op=ALU.mult),
                   waits=[(SE, kev), cc])
            pr.add("vector", lambda e: e.tensor_tensor(out=tmpB[:], in0=sn_sb[:], in1=p1[:, 0:TT],
                                                       op=ALU.mult),
                   waits=[(TE, rot)])
            krope = pr.add("vector", lambda e, i_=i: e.tensor_tensor(
                out=kr_sb[:, i_ * TT:(i_ + 1) * TT], in0=tmpA[:], in1=tmpB[:], op=ALU.add),
                inc=VE)
            p1_free = krope
            cs_free = krope

            vmm = None
            for tc in range(TT // 128):
                for dc in range(DC):
                    vmm = pr.add("tensor",
                                 lambda e, d=dc, t_=tc: e.matmul(
                                     p2[:, t_ * 128:(t_ + 1) * 128],
                                     xn_sb[:, d, t_ * 128:(t_ + 1) * 128],
                                     wbufs[1][:, d],
                                     start=(d == 0), stop=(d == DC - 1)),
                                 waits=[ldw_blk[1], (VE, xn_done), (VE, p2_free)],
                                 inc=TE if dc == DC - 1 else None)
            if i == NSUB - 1:
                freew_blk[1] = vmm
                wload_next()
            vev = pr.add("vector", lambda e, i_=i: e.tensor_copy(v_sb[:, i_ * TT:(i_ + 1) * TT],
                                                                 p2[:, 0:TT]),
                         waits=[(TE, vmm)], inc=VE)
            p2_free = vev
            kv_ready = vev
            if DEBUG_EARLY and i == 0:
                for nm, apsrc, gate in (
                        ("e_xn", xn_sb[:], xn_done), ("e_kr", kr_sb[:, 0:TT], krope),
                        ("e_v", v_sb[:, 0:TT], vev)):
                    shp = [int(s) for s in apsrc.shape]
                    dap = nc.dram_tensor("dbg_" + nm, shp, F32, kind="ExternalOutput").ap()
                    pr.add("gpsimd",
                           (lambda d_, s_: (lambda e: e.dma_start(out=d_, in_=s_)))(dap, apsrc),
                           waits=[(VE, gate)], inc=DBG, inck=16)

        kblk = [2]

        def consume_block():
            k = kblk[0]
            kblk[0] += 1
            return k

        sig_cnt = [0, 0]
        asc_cnt = [0, 0]
        free_bank = [p0_free, 0]
        last_smm = 0
        vmm_slot = [0, 0]
        c1_cnt = [0, 0]
        p3b_free = 0
        rope_cnt = 0
        qevac = p0_free
        aoev = 0
        yevac_c = [0, 0]
        yadd_c = [0, 0]
        ydma_c = [0, 0]
        hw_cnt = 0
        out_dmas = 0

        for t in range(NT_BUILD):
            sub = t + 1
            xc = load_x(sub, [(VE, x_free)])
            cc = load_cs(sub, [(VE, cs_free)])
            st["p3s_free"] = p3b_free
            xn_done, _ = emit_ln(xc, extra_bcast_waits=[(VE, aoev)] if aoev else [],
                                 p5_waits=[(VE, st["p3v_free"]), (VE, hw_cnt)])

            pQb = [p0, p6]
            kq0 = consume_block()
            qmm_cur = None
            for dc in range(DC):
                qmm_cur = pr.add("tensor",
                                 lambda e, d=dc, k_=kq0: e.matmul(pQb[0][:, 0:TT],
                                                                  wbufs[k_ % 3][:, d],
                                                                  xn_sb[:, d],
                                                                  start=(d == 0), stop=(d == DC - 1)),
                                 waits=[ldw_blk[kq0], (VE, xn_done), (SE, free_bank[0])],
                                 inc=TE if dc == DC - 1 else None)
            freew_blk[kq0] = qmm_cur
            wload_next()
            for h in range(DBG_HEADS):
                bank = h % 2
                nbank = (h + 1) % 2
                kqn = consume_block() if h + 1 < DBG_HEADS else None
                qmm_next = None
                qevac = pr.add("scalar", lambda e, b_=bank: e.activation(out=qt1[:],
                                                                         in_=pQb[b_][:, 0:TT],
                                                                         func=AF.Copy),
                               waits=[(TE, qmm_cur), (VE, rope_cnt)], inc=SE)
                free_bank[bank] = qevac
                rot = pr.add("tensor", lambda e: e.matmul(p5[:, 0:TT], rm_sb[:], qt1[:],
                                                          start=True, stop=True),
                             waits=[(SE, qevac), (VE, hw_cnt), (VE, rope_cnt),
                                    (VE, st["p3v_free"])], inc=TE)
                pr.add("vector", lambda e: e.tensor_tensor(out=tmpA[:], in0=qt1[:], in1=cs_sb[:],
                                                           op=ALU.mult),
                       waits=[(SE, qevac), cc])
                pr.add("vector", lambda e: e.tensor_tensor(out=tmpB[:], in0=sn_sb[:],
                                                           in1=p5[:, 0:TT], op=ALU.mult),
                       waits=[(TE, rot)])
                rope_cnt = pr.add("vector", lambda e: e.tensor_tensor(out=qr_sb[:], in0=tmpA[:],
                                                                      in1=tmpB[:], op=ALU.add),
                                  waits=[(TE, last_smm)], inc=VE)

                for qc in range(2):
                    for w in range(2):
                        ss = (qc * 2 + w) % 2
                        koff = t * TT + w * TT
                        cidx = t * 4 + qc * 2 + w
                        smm = pr.add("tensor",
                                     lambda e, q_=qc, ko=koff, s_=ss: e.matmul(
                                         pS[s_][:, 0:512],
                                         qr_sb[:, q_ * 128:(q_ + 1) * 128],
                                         kr_sb[:, ko:ko + 512],
                                         start=True, stop=True),
                                     waits=[(VE, rope_cnt), (VE, kv_ready), (SE, sig_cnt[ss])],
                                     inc=TE)
                        last_smm = smm
                        if kqn is not None:
                            i4 = qc * 2 + w
                            for dc in range(i4 * 8, (i4 + 1) * 8):
                                qmm_next = pr.add(
                                    "tensor",
                                    lambda e, d=dc, k_=kqn, b_=nbank: e.matmul(
                                        pQb[b_][:, 0:TT], wbufs[k_ % 3][:, d], xn_sb[:, d],
                                        start=(d == 0), stop=(d == DC - 1)),
                                    waits=[ldw_blk[kqn], (VE, xn_done),
                                           (SE, free_bank[nbank])],
                                    inc=TE if dc == DC - 1 else None)
                            if i4 == 3:
                                freew_blk[kqn] = qmm_next
                                wload_next()
                        pr.add("scalar",
                               lambda e, s_=ss: e.activation(out=an[s_][:],
                                                             in_=pS[s_][:, 0:512],
                                                             func=AF.Sigmoid, scale=SCALE,
                                                             accum_out=scol[:, s_:s_ + 1]),
                               waits=[(TE, smm), (VE, asc_cnt[ss]), (VE, c1_cnt[ss])])
                        sgc = pr.add("scalar",
                                     lambda e, s_=ss: e.activation(out=sdum[:],
                                                                   in_=scol[:, s_:s_ + 1],
                                                                   func=AF.Copy),
                                     inc=SE)
                        sig_cnt[ss] = sgc
                        c1_cnt[ss] = pr.add("vector",
                                            lambda e, s_=ss, ci=cidx: e.tensor_scalar(
                                                out=ctmp[:], in0=scol[:, s_:s_ + 1],
                                                scalar1=su_sb[:, ci:ci + 1], scalar2=None,
                                                op0=ALU.subtract),
                                            waits=[(SE, sgc), ldw_c["su"]], inc=VE)
                        pr.add("vector", lambda e: e.reciprocal(ctmp2[:], ctmp[:]))
                        ccd = pr.add("vector",
                                     lambda e, ci=cidx: e.tensor_tensor(out=ccol[:], in0=ctmp2[:],
                                                                        in1=cf_sb[:, ci:ci + 1],
                                                                        op=ALU.mult),
                                     waits=[ldw_c["cf"]], inc=VE)
                        ac = pr.add("vector",
                                    lambda e, s_=ss: e.tensor_scalar(out=asc[s_][:], in0=an[s_][:],
                                                                     scalar1=ccol[:], scalar2=None,
                                                                     op0=ALU.mult),
                                    waits=[(VE, ccd)], inc=VE)
                        asc_cnt[ss] = ac
                        trc = None
                        for j in range(4):
                            trc = pr.add("tensor",
                                         lambda e, j_=j, s_=ss: e.transpose(
                                             p3[:, j_ * 128:(j_ + 1) * 128],
                                             asc[s_][:, j_ * 128:(j_ + 1) * 128],
                                             im_sb[:]),
                                         waits=[(VE, ac), (SE, p3b_free), ldw_c["im"]],
                                         inc=TE if j == 3 else None)
                        aTe = pr.add("scalar",
                                     lambda e, s_=ss: e.activation(out=aT[s_][:], in_=p3[:, 0:512],
                                                                   func=AF.Copy),
                                     waits=[(TE, trc), (TE, vmm_slot[ss])], inc=SE)
                        p3b_free = aTe
                        vmm = None
                        for j in range(4):
                            tc = koff // 128 + j
                            vmm = pr.add("tensor",
                                         lambda e, j_=j, t_=tc, q_=qc, s_=ss, w_=w: e.matmul(
                                             p4[:, q_ * 128:(q_ + 1) * 128],
                                             v_sb[:, t_ * 128:(t_ + 1) * 128],
                                             aT[s_][:, j_ * 128:(j_ + 1) * 128],
                                             start=(w_ == 0 and j_ == 0),
                                             stop=(w_ == 1 and j_ == 3)),
                                         waits=[(SE, aTe), (VE, aoev)],
                                         inc=TE if (w == 1 and j == 3) else None)
                        if vmm is not None:
                            vmm_slot[ss] = vmm
                aoev = pr.add("vector", lambda e, h_=h: e.tensor_copy(ao_sb[:, h_], p4[:, 0:TT]),
                              waits=[(TE, vmm_slot[1])], inc=VE)
                qmm_cur = qmm_next
            if not DBG_FFN:
                x_free = xn_done
                continue
            lgmm = None
            for dc in range(DC):
                lgmm = pr.add("tensor",
                              lambda e, d=dc: e.matmul(p5[0:8, 0:TT], wr_sb[:, d], xn_sb[:, d],
                                                       start=(d == 0), stop=(d == DC - 1)),
                              waits=[ldw_c["wr"], (VE, xn_done), (VE, hw_cnt),
                                     (VE, rope_cnt)],
                              inc=TE if dc == DC - 1 else None)
            lgev = pr.add("scalar", lambda e: e.activation(out=lg_sb[:], in_=p5[0:8, 0:TT],
                                                           func=AF.Copy),
                          waits=[(TE, lgmm)], inc=SE)
            wtd = 0
            wtr = None
            for qc in range(2):
                ltr = pr.add("tensor",
                             lambda e, q_=qc: e.transpose(p3[:, q_ * 8:(q_ + 1) * 8],
                                                          lg_sb[0:8, q_ * 128:(q_ + 1) * 128],
                                                          im_sb[0:8, 0:8]),
                             waits=[(SE, lgev), (SE, p3b_free), (VE, wtd)], inc=TE)
                lgt = lgt2[qc]
                cpd = pr.add("vector", lambda e, q_=qc, g_=lgt: e.tensor_copy(
                    g_[:], p3[:, q_ * 8:(q_ + 1) * 8]),
                       waits=[(TE, ltr)], inc=VE)
                m1d = pr.add("vector", lambda e, g_=lgt: e.reduce_max(m1c[:], g_[:], axis=AX.X),
                             waits=[(VE, cpd)], inc=VE)
                mk1d = pr.add("vector", lambda e, g_=lgt: e.tensor_scalar(
                    out=mk1[:], in0=g_[:], scalar1=m1c[:], scalar2=None, op0=ALU.is_equal),
                       waits=[(VE, m1d)], inc=VE)
                tbd = pr.add("vector", lambda e: e.tensor_scalar(out=tb8[:], in0=mk1[:],
                                                                 scalar1=-1e30,
                                                                 scalar2=None, op0=ALU.mult),
                             waits=[(VE, mk1d)], inc=VE)
                l2d = pr.add("vector", lambda e, g_=lgt: e.tensor_tensor(out=l2[:], in0=tb8[:],
                                                                         in1=g_[:], op=ALU.add),
                             waits=[(VE, tbd)], inc=VE)
                m2d = pr.add("vector", lambda e: e.reduce_max(m2c[:], l2[:], axis=AX.X),
                             waits=[(VE, l2d)], inc=VE)
                dcd = pr.add("vector", lambda e: e.tensor_tensor(out=dcol[:], in0=m1c[:],
                                                                 in1=m2c[:], op=ALU.subtract),
                             waits=[(VE, m2d)], inc=VE)
                mkd = pr.add("vector", lambda e: e.tensor_scalar(out=mk2[:], in0=l2[:],
                                                                 scalar1=m2c[:], scalar2=None,
                                                                 op0=ALU.is_equal),
                             waits=[(VE, dcd)], inc=VE)
                ppd = pr.add("scalar",
                             lambda e: (e.activation(out=p1c[:], in_=dcol[:], func=AF.Sigmoid),
                                        e.activation(out=p2c[:], in_=dcol[:], func=AF.Sigmoid,
                                                     scale=-1.0))[-1],
                             waits=[(VE, mkd)], inc=SE)
                w1d = pr.add("vector", lambda e: e.tensor_scalar(out=w1c[:], in0=mk1[:],
                                                                 scalar1=p1c[:],
                                                                 scalar2=None, op0=ALU.mult),
                             waits=[(SE, ppd)], inc=VE)
                w2d = pr.add("vector", lambda e: e.tensor_scalar(out=w2c[:], in0=mk2[:],
                                                                 scalar1=p2c[:],
                                                                 scalar2=None, op0=ALU.mult),
                             waits=[(VE, w1d)], inc=VE)
                wtd = pr.add("vector", lambda e: e.tensor_tensor(out=wtok[:], in0=w1c[:],
                                                                 in1=w2c[:], op=ALU.add),
                             waits=[(VE, w2d)], inc=VE)
                wtr = pr.add("tensor",
                             lambda e, q_=qc: e.transpose(
                                 p3[0:8, 128 + q_ * 128:128 + (q_ + 1) * 128],
                                 wtok[:], im_sb[:]),
                             waits=[(VE, wtd)], inc=TE)
            wfev = pr.add("scalar", lambda e: e.activation(out=wfm[:], in_=p3[0:8, 128:128 + TT],
                                                           func=AF.Copy),
                          waits=[(TE, wtr)], inc=SE)
            p3b_free = wfev
            wxm = pr.add("tensor", lambda e: e.matmul(p5[:, 0:TT], em_sb[:], wfm[:],
                                                      start=True, stop=True),
                         waits=[(SE, wfev), ldw_c["em"]], inc=TE)
            wexp_done = pr.add("vector", lambda e: e.tensor_copy(wexp_sb[:], p5[:, 0:TT]),
                               waits=[(TE, wxm)], inc=VE)
            kat = consume_block()
            hmm_c = None
            for dc in range(DC):
                hmm_c = pr.add("tensor",
                               lambda e, d=dc, k_=kat: e.matmul(p5[:, TT:2 * TT],
                                                                wbufs[k_ % 3][:, d],
                                                                xn_sb[:, d],
                                                                start=(d == 0), stop=(d == DC - 1)),
                               waits=[ldw_blk[kat], (VE, wexp_done)],
                               inc=TE if dc == DC - 1 else None)
            freew_blk[kat] = hmm_c
            wload_next()
            hw_cnt = pr.add("vector", lambda e: e.tensor_tensor(out=hw_sb[:], in0=p5[:, TT:2 * TT],
                                                                in1=wexp_sb[:], op=ALU.mult),
                            waits=[(TE, hmm_c)], inc=VE)

            if not DBG_Y:
                x_free = xn_done
                continue
            yadd_last = None
            for oc in range(DC):
                kwo = consume_block()
                kwb = consume_block()
                pp = oc % 2
                pr.add("tensor",
                       lambda e, k_=kwo, p_=pp: e.matmul(pY[p_][:, 0:TT],
                                                         wbufs[k_ % 3][:, DC], hw_sb[:],
                                                         start=True, stop=False),
                       waits=[ldw_blk[kwo], (VE, hw_cnt), (SE, yevac_c[pp]),
                              (SE, free_bank[1] if pp == 0 else 0)])
                wo_last = None
                for dc in range(DC):
                    wo_last = pr.add("tensor",
                                     lambda e, d=dc, k_=kwo, p_=pp: e.matmul(
                                         pY[p_][:, 0:TT], wbufs[k_ % 3][:, d], ao_sb[:, d],
                                         start=False, stop=False),
                                     waits=[(VE, aoev)],
                                     inc=TE if dc == DC - 1 else None)
                freew_blk[kwo] = wo_last
                wload_next()
                wb_last = None
                for dc in range(DC):
                    wb_last = pr.add("tensor",
                                     lambda e, d=dc, k_=kwb, p_=pp: e.matmul(
                                         pY[p_][:, 0:TT], wbufs[k_ % 3][:, d], xn_sb[:, d],
                                         start=False, stop=(d == DC - 1)),
                                     waits=[ldw_blk[kwb]],
                                     inc=TE if dc == DC - 1 else None)
                ymm = wb_last
                freew_blk[kwb] = ymm
                wload_next()
                yev = pr.add("scalar",
                             lambda e, o_=oc, p_=pp: e.activation(out=ya[p_][:],
                                                                  in_=pY[p_][:, 0:TT],
                                                                  func=AF.Identity,
                                                                  bias=bb_sb[:, o_:o_ + 1]),
                             waits=[(TE, ymm), (VE, yadd_c[pp]), ldw_c["bb"]], inc=SE)
                yevac_c[pp] = yev
                yad = pr.add("vector",
                             lambda e, o_=oc, p_=pp: e.tensor_tensor(out=yb[p_][:], in0=ya[p_][:],
                                                                     in1=x_sb[:, o_], op=ALU.add),
                             waits=[(SE, yev), (STOs[pp], ydma_c[pp])], inc=VE)
                yadd_c[pp] = yad
                yadd_last = yad
                if pp == 0:
                    free_bank[1] = yev
                ydma_c[pp] = pr.add("gpsimd",
                                    lambda e, t_=t, o_=oc, p_=pp: e.dma_start(out=y_out[t_, o_],
                                                                              in_=yb[p_][:]),
                                    waits=[(VE, yad)], inc=STOs[pp], inck=16)
                out_dmas += 1
            x_free = yadd_last

        if DEBUG_DUMP:
            dbg_bufs = dict(xn=xn_sb, kr=kr_sb, v=v_sb, ao=ao_sb, x=x_sb,
                            can1=can1, can2=can2, can3=can3,
                            an0=an[0], an1=an[1], asc0=asc[0], asc1=asc[1],
                            aT0=aT[0], aT1=aT[1], scol=scol, ccol=ccol,
                            lgt0=lgt2[0], lgt1=lgt2[1], mk1=mk1, mk2=mk2, l2=l2,
                            rr=rr_t, m2r=m2_t, ra=ra, rb=rb,
                            qr=qr_sb, hw=hw_sb, wexp=wexp_sb, lg=lg_sb,
                            wtok=wtok, m1c=m1c, m2c=m2c, p1c=p1c, p2c=p2c)
            endw = [(TE, TE.n), (SE, SE.n), (VE, VE.n)]
            for nm, buf in dbg_bufs.items():
                shp = [int(s) for s in buf[:].shape]
                dap = nc.dram_tensor("dbg_" + nm, shp, F32, kind="ExternalOutput").ap()
                pr.add("gpsimd",
                       (lambda d_, b_: (lambda e: e.dma_start(out=d_, in_=b_[:])))(dap, buf),
                       waits=endw, inc=DBG, inck=16)
        fin = [(STOs[0], STOs[0].n), (STOs[1], STOs[1].n)]
        if DBG.n:
            fin.append((DBG, DBG.n))
        pr.add("gpsimd", None, waits=fin)

        block = ctx.enter_context(nc.Block())

        @block.sync
        def _(e):
            pr.emit("sync", e)

        @block.tensor
        def _(e):
            pr.emit("tensor", e)

        @block.scalar
        def _(e):
            pr.emit("scalar", e)

        @block.vector
        def _(e):
            pr.emit("vector", e)

        @block.gpsimd
        def _(e):
            pr.emit("gpsimd", e)

    return nc



def _rope_tables():
    base = 10000.0
    inv_freq = 1.0 / base ** (np.arange(0, DH, 2, dtype=np.float32) / DH)
    t = np.arange(S, dtype=np.float32)
    f = np.outer(t, inv_freq)
    cos = np.zeros((S, DH), dtype=np.float32)
    sin = np.zeros((S, DH), dtype=np.float32)
    cos[:, 0::2] = np.cos(f)
    cos[:, 1::2] = np.cos(f)
    sin[:, 0::2] = np.sin(f)
    sin[:, 1::2] = np.sin(f)
    return cos.T.copy(), sin.T.copy()


def _pack_wt(wt):
    m = wt.shape[1]
    return np.ascontiguousarray(wt.reshape(DC, 128, m).transpose(1, 0, 2))


def prep_in_maps(L0_attn_x, L0_attn_Wq, L0_attn_Wk, L0_attn_Wv, L0_attn_Wo,
                 L0_ffn_W_base, L0_ffn_b_base, L0_ffn_W_router, L0_ffn_A, L0_ffn_B):
    f32 = np.float32
    x = np.asarray(L0_attn_x, dtype=f32)
    Wq = np.asarray(L0_attn_Wq, dtype=f32)
    Wk = np.asarray(L0_attn_Wk, dtype=f32)
    Wv = np.asarray(L0_attn_Wv, dtype=f32)
    Wo = np.asarray(L0_attn_Wo, dtype=f32)
    Wb = np.asarray(L0_ffn_W_base, dtype=f32)
    bb = np.asarray(L0_ffn_b_base, dtype=f32)
    Wr = np.asarray(L0_ffn_W_router, dtype=f32)
    A = np.asarray(L0_ffn_A, dtype=f32)
    Bm = np.asarray(L0_ffn_B, dtype=f32)

    wq_p = np.ascontiguousarray(
        Wq.T.reshape(DC, 128, DC, 128).transpose(2, 1, 0, 3))
    wo_p4 = Wo.T.reshape(DC, 128, DC, 128).transpose(2, 1, 0, 3)
    bfT = Bm.transpose(0, 2, 1).reshape(128, D)
    wo_p = np.zeros((DC, 128, DC + 1, 128), dtype=f32)
    wo_p[:, :, :DC, :] = wo_p4
    for oc in range(DC):
        wo_p[oc, :, DC, :] = bfT[:, oc * 128:(oc + 1) * 128]
    wo_p = np.ascontiguousarray(wo_p)
    wb_p = np.ascontiguousarray(
        Wb.T.reshape(DC, 128, DC, 128).transpose(2, 1, 0, 3))
    wk_p = _pack_wt(np.ascontiguousarray(Wk.T))
    wv_p = _pack_wt(np.ascontiguousarray(Wv.T))
    at_p = _pack_wt(np.ascontiguousarray(A.reshape(128, D).T))
    wr_p = _pack_wt(np.ascontiguousarray(Wr.T))
    bb_p = np.ascontiguousarray(bb.reshape(DC, 128).T)

    rm = np.zeros((128, 128), dtype=f32)
    half = DH // 2
    for j in range(half):
        rm[half + j, j] = -1.0
        rm[j, half + j] = 1.0
    im = np.eye(128, dtype=f32)
    em = np.zeros((8, 128), dtype=f32)
    for e_ in range(8):
        em[e_, e_ * 16:(e_ + 1) * 16] = 2.0
    ones1 = np.ones((1, 128), dtype=f32)
    onesc = np.ones((128, 1), dtype=f32)

    cosT, sinT = _rope_tables()
    alpha64 = np.linspace(0.0, 1.0, 256)

    in_maps = []
    for c in range(NCORES):
        b = c // 4
        q0 = (c % 4) * TOK
        lo = q0 - HALO
        arr = np.zeros((AR, D), dtype=f32)
        s0 = max(lo, 0)
        s1 = min(lo + AR, S)
        arr[s0 - lo:s1 - lo] = x[b, s0:s1]
        xs_p = np.ascontiguousarray(
            arr.reshape(NSUB, TT, DC, 128).transpose(0, 3, 2, 1))

        cos_p = np.zeros((NSUB, 128, TT), dtype=f32)
        sin_p = np.zeros((NSUB, 128, TT), dtype=f32)
        cslc = np.zeros((128, AR), dtype=f32)
        snlc = np.zeros((128, AR), dtype=f32)
        cslc[:, s0 - lo:s1 - lo] = cosT[:, s0:s1]
        snlc[:, s0 - lo:s1 - lo] = sinT[:, s0:s1]
        for i in range(NSUB):
            cos_p[i] = cslc[:, i * TT:(i + 1) * TT]
            sin_p[i] = snlc[:, i * TT:(i + 1) * TT]

        seq_start = (c % 4 == 0)
        seq_end = (c % 4 == 3)
        coef = np.zeros((128, NT * 4), dtype=f32)
        subc = np.zeros((128, NT * 4), dtype=f32)
        j = np.arange(128)
        for t in range(NT):
            for qc in range(2):
                aa = alpha64[qc * 128 + j]
                idx = t * 4 + qc * 2
                if seq_start and t == 0:
                    coef[:, idx] = 0.0
                    coef[:, idx + 1] = 1.0
                else:
                    coef[:, idx] = (1.0 - aa).astype(f32)
                    coef[:, idx + 1] = aa.astype(f32)
                if seq_end and t == NT - 1:
                    subc[:, idx + 1] = 128.0
        in_maps.append(dict(
            xs_p=xs_p, cos_p=cos_p, sin_p=sin_p,
            wk_p=wk_p, wv_p=wv_p, wq_p=wq_p, wo_p=wo_p, wb_p=wb_p,
            at_p=at_p, wr_p=wr_p, bb_p=bb_p,
            coef_p=coef, sub_p=subc,
            rm_p=rm, im_p=im, em_p=em, ones1_p=ones1, onesc_p=onesc,
            eps_p=np.full((1, 1), EPS, dtype=f32),
            can_p=np.full((128, 256), 777.0, dtype=f32),
        ))
    return in_maps


def kernel(**inputs):
    in_maps = prep_in_maps(**inputs)
    nc = build_nc()
    res = run_bass_kernel_spmd(nc, in_maps, core_ids=list(range(NCORES)))
    out = np.empty((NCORES * TOK, D), dtype=np.float32)
    for c in range(NCORES):
        a = np.asarray(res.results[c]["y_out"])
        out[c * TOK:(c + 1) * TOK] = a.transpose(0, 3, 1, 2).reshape(TOK, D)
    return out.reshape(B, S, D)
